# version 01
# baseline (speedup 1.0000x reference)
import numpy as np
import ml_dtypes
import concourse.bass as bass
import concourse.mybir as mybir
from concourse.bass_utils import run_bass_kernel_spmd

F32 = mybir.dt.float32
BF16 = mybir.dt.bfloat16
AF = mybir.ActivationFunctionType
ALU = mybir.AluOpType
AX = mybir.AxisListType

B, S, D = 4, 4096, 2048
HPC, DH = 8, 128
TPC = 2048
NE, DI = 32, 768
STW = 512
NST = S // STW
C_EXP = 8.0
EPS = 1e-5
QSCALE = 1.0 / float(np.sqrt(np.float32(128.0)))
NEG = -1e10

ENGS = ("pe", "dve", "act", "pool", "sp")
PHASES = ["PA", "PB", "PC", "PD", "PE", "F1", "F2", "PG"]


class Op:
    __slots__ = ("eng", "fn", "reads", "writes", "kind", "idx", "eseq",
                 "waits", "needs_inc", "inc_sem", "inc_amt", "tag")

    def __init__(self, eng, fn, reads, writes, kind, idx, eseq, tag):
        self.eng, self.fn = eng, fn
        self.reads, self.writes = reads, writes
        self.kind = kind
        self.idx, self.eseq = idx, eseq
        self.waits = []
        self.needs_inc = False
        self.inc_sem = None
        self.inc_amt = 0
        self.tag = tag


class Prog:

    def __init__(self):
        self.ops = []
        self.streams = {e: [] for e in ENGS}
        self.regs = {}
        self.clock = {e: {} for e in ENGS}
        self.last_comp = {e: None for e in ENGS}
        self.dma_count = {"sp": 0, "pool": 0}
        self.cc_count = 0

    def region(self, r):
        st = self.regs.get(r)
        if st is None:
            st = [None, []]
            self.regs[r] = st
        return st

    def new_region(self, r, preds=()):
        self.regs[r] = [None, [p for p in preds if p is not None]]

    def _edge(self, op, prod):
        if prod is op:
            return
        e, f = op.eng, prod.eng
        if e == "pe" and f == "pe":
            return
        if self.clock[e].get(f, -1) >= prod.eseq:
            return
        if prod.kind in ("d", "cc"):
            op.waits.append((prod.inc_sem, prod.inc_amt))
            self.clock[e][f] = prod.eseq
            return
        if e == f:
            sub = self.last_comp[e]
            assert sub is not None and sub.eseq >= prod.eseq
            sub.needs_inc = True
            op.waits.append(sub)
            self.clock[e][f] = sub.eseq
        else:
            prod.needs_inc = True
            op.waits.append(prod)
            self.clock[e][f] = prod.eseq

    def emit(self, eng, fn, reads=(), writes=(), kind="c", tag=""):
        idx = len(self.ops)
        op = Op(eng, fn, list(reads), list(writes), kind, idx,
                len(self.streams[eng]), tag)
        self.ops.append(op)
        self.streams[eng].append(op)
        if kind == "d":
            self.dma_count[eng] += 1
            op.inc_sem = "sd" if eng == "sp" else "pd"
            op.inc_amt = 16 * self.dma_count[eng]
            op.needs_inc = True
        elif kind == "cc":
            self.cc_count += 1
            op.inc_sem = "cc"
            op.inc_amt = self.cc_count
            op.needs_inc = True
        prods = {}
        for r in op.reads:
            st = self.region(r)
            if st[0] is not None:
                prods[st[0].idx] = st[0]
        for r in op.writes:
            st = self.region(r)
            if st[0] is not None:
                prods[st[0].idx] = st[0]
            for rd in st[1]:
                prods[rd.idx] = rd
        for pidx in sorted(prods):
            self._edge(op, prods[pidx])
        for r in op.reads:
            self.region(r)[1].append(op)
        for r in op.writes:
            st = self.region(r)
            st[0] = op
            st[1] = []
        if kind == "c":
            self.last_comp[eng] = op
        return op

    def finalize(self):
        inc_no = {}
        totals = {}
        for e in ENGS:
            n = 0
            for op in self.streams[e]:
                if op.kind == "c" and op.needs_inc:
                    n += 1
                    op.inc_sem = e
                    inc_no[op.idx] = n
            totals[e] = n
        for op in self.ops:
            byname = {}
            for w in op.waits:
                if isinstance(w, tuple):
                    s, v = w
                else:
                    s, v = w.eng, inc_no[w.idx]
                if byname.get(s, 0) < v:
                    byname[s] = v
            op.waits = sorted(byname.items())
        totals["sd"] = 16 * self.dma_count["sp"]
        totals["pd"] = 16 * self.dma_count["pool"]
        totals["cc"] = self.cc_count
        for op in self.ops:
            for s, v in op.waits:
                assert v <= totals.get(s, 0), f"wait {s}>={v} exceeds total {totals.get(s)} ({op.tag})"
        return totals


class Arena:
    def __init__(self, prog, size):
        self.P = prog
        self.size = size
        self.cur = 0
        self.live = []
        self.marks = []
        self.freed = []

    def alloc(self, name, n):
        a, b = self.cur, self.cur + n
        assert b <= self.size, f"arena overflow: {name}: {b} > {self.size}"
        self.cur = b
        self.live.append((name, a, b))
        preds = []
        st_old = self.P.regs.get(name)
        if st_old is not None:
            preds += ([st_old[0]] if st_old[0] else []) + st_old[1]
        for (nm, fa, fb, fp) in self.freed:
            if fa < b and a < fb:
                preds += fp
        self.P.new_region(name, preds)
        return (a, b)

    def mark(self):
        self.marks.append((self.cur, list(self.live)))

    def release(self):
        cur0, live0 = self.marks.pop()
        for (nm, a, b) in self.live:
            if (nm, a, b) not in live0:
                st = self.P.regs.get(nm, [None, []])
                preds = ([st[0]] if st[0] else []) + st[1]
                self.freed.append((nm, a, b, preds))
        self.cur = cur0
        self.live = live0


def _host_consts():
    ident = np.eye(128, dtype=ml_dtypes.bfloat16)
    mask = np.zeros((128, 4 * 512), dtype=np.float32)
    for j in range(4):
        i = np.arange(128)[:, None]
        kk = np.arange(512)[None, :]
        mask[:, j * 512:(j + 1) * 512] = np.where(kk <= 128 * j + i, 0.0, NEG)
    return ident, mask.astype(ml_dtypes.bfloat16)


def build_program(upto="PG", taps=()):
    from contextlib import ExitStack
    PH = PHASES.index(upto)
    taps = set(taps)
    nc = bass.Bass("TRN2", target_bir_lowering=False)
    P = Prog()

    x_in = nc.dram_tensor("x", [S, D], F32, kind="ExternalInput").ap()
    xown_in = nc.dram_tensor("xown", [TPC, D], F32, kind="ExternalInput").ap()
    wqkv_in = nc.dram_tensor("wqkv", [D, 3072], F32, kind="ExternalInput").ap()
    wout_in = nc.dram_tensor("wout", [D, D], F32, kind="ExternalInput").ap()
    gate_in = nc.dram_tensor("gate", [D, NE], F32, kind="ExternalInput").ap()
    gup_in = nc.dram_tensor("gup", [NE, D, 2 * DI], F32, kind="ExternalInput").ap()
    gdn_in = nc.dram_tensor("gdn", [NE, DI, D], F32, kind="ExternalInput").ap()
    ident_in = nc.dram_tensor("ident", [128, 128], BF16, kind="ExternalInput").ap()
    maskc_in = nc.dram_tensor("maskc", [128, 2048], BF16, kind="ExternalInput").ap()
    out_ext = nc.dram_tensor("out", [TPC, D], F32, kind="ExternalOutput").ap()

    tap_ext = {}
    tap_shapes = {
        "qT": ([HPC, 128, S], BF16), "kT": ([HPC, 128, S], BF16),
        "v": ([S, HPC * DH], BF16), "ctx": ([HPC * 128, S], BF16),
        "y": ([TPC, D], F32), "rw": ([TPC, NE], F32),
        "gated": ([2, DI, TPC], BF16), "acc": ([TPC, D], F32),
        "ctx2": ([16 * 128, TPC], BF16),
    }
    for t in taps:
        tap_ext[t] = nc.dram_tensor("tap_" + t, *tap_shapes[t], kind="ExternalOutput").ap()

    xbf_h = nc.dram_tensor("xbf_h", [S, D], BF16).ap()
    qT_h = nc.dram_tensor("qT_h", [HPC, 128, S], BF16).ap()
    kT_h = nc.dram_tensor("kT_h", [HPC, 128, S], BF16).ap()
    v_h = nc.dram_tensor("v_h", [S, HPC * DH], BF16).ap()
    y_h = nc.dram_tensor("y_h", [TPC, D], F32).ap()
    ybf_h = nc.dram_tensor("ybf_h", [TPC, D], BF16).ap()
    gated_h = nc.dram_tensor("gated_h", [NE, DI, TPC], BF16).ap()
    ccin = nc.dram_tensor("ccin", [HPC * 128, S // 2], F32)
    ccout = nc.dram_tensor("ccout", [2 * HPC * 128, S // 2], F32)

    with ExitStack() as ctx:
        E = ctx.enter_context
        AR = 94208
        arena = E(nc.sbuf_tensor("arena", [128, AR], BF16))
        ident = E(nc.sbuf_tensor("identt", [128, 128], BF16))
        maskc = E(nc.sbuf_tensor("maskct", [128, 2048], BF16))
        RW = E(nc.sbuf_tensor("RWt", [128, 512], F32))
        stats = E(nc.sbuf_tensor("statst", [128, 16], F32))
        sumblk = E(nc.sbuf_tensor("sumblkt", [128, 48], F32))
        rt = E(nc.sbuf_tensor("rtt", [128, 160], F32))
        stage = E(nc.sbuf_tensor("staget", [128, 4 * 512], BF16))
        sqscr = E(nc.sbuf_tensor("sqscrt", [128, 2048], F32))
        cbias = E(nc.sbuf_tensor("cbiast", [128, 2], F32))
        psum = E(nc.psum_tensor("psumt", [128, 4096], F32))

        sems = {}
        for snm in ("pe", "dve", "act", "pool", "sd", "pd", "cc"):
            sems[snm] = E(nc.semaphore("sem_" + snm))
        r_po = E(nc.gpsimd.register("r_po"))
        r_po2 = E(nc.gpsimd.register("r_po2"))
        block = E(nc.Block())

        A = Arena(P, AR)

        def ps(i, n=1):
            return psum.ap()[:, i * 512:(i + n) * 512]

        def psb4(i):
            return psum.ap()[:, i * 512:(i + 1) * 512].bitcast(BF16)[:, 0:512]

        PSR = [f"ps{i}" for i in range(8)]

        def bf(rng):
            return arena.ap()[:, rng[0]:rng[1]]

        def f32(rng):
            return arena.ap()[:, rng[0]:rng[1]].bitcast(F32)

        def ln_chain(src_ap, src_reg, dst_ap, dst_reg):
            P.emit("dve", lambda o: o.tensor_scalar_mul(stats[:, 1:2], stats[:, 0:1], -1.0 / 2048.0),
                   reads=["lnstats"], writes=["lnstats"], tag="lnm")
            P.emit("dve", lambda o: o.tensor_scalar_add(src_ap, src_ap, stats[:, 1:2]),
                   reads=["lnstats", src_reg], writes=[src_reg], tag="lnt")
            P.emit("act", lambda o: o.activation(sqscr[:, :], src_ap, AF.Square,
                                                 accum_out=stats[:, 2:3]),
                   reads=[src_reg], writes=["sqscr", "lnstats"], tag="lnsq")
            P.emit("act", lambda o: o.activation(stats[:, 3:4], stats[:, 2:3], AF.Ln,
                                                 scale=1.0 / 2048.0, bias=cbias[:, 1:2]),
                   reads=["lnstats", "cbias"], writes=["lnstats"], tag="lnln")
            P.emit("act", lambda o: o.activation(stats[:, 4:5], stats[:, 3:4], AF.Exp, scale=-0.5),
                   reads=["lnstats"], writes=["lnstats"], tag="lnexp")
            P.emit("dve", lambda o: o.tensor_scalar_mul(dst_ap, src_ap, stats[:, 4:5]),
                   reads=["lnstats", src_reg], writes=[dst_reg], tag="lny")

        P.emit("dve", lambda o: o.memset(cbias[:, 0:1], -C_EXP),
               writes=["cbias"], tag="cb0")
        P.emit("dve", lambda o: o.memset(cbias[:, 1:2], EPS),
               writes=["cbias"], tag="cb1")
        P.emit("sp", lambda o: o.dma_start(ident[:, :], ident_in),
               writes=["ident"], kind="d", tag="ld_ident")
        P.emit("sp", lambda o: o.dma_start(maskc[:, :], maskc_in),
               writes=["maskc"], kind="d", tag="ld_mask")
        for i in range(8):
            sl = slice(i * 512, (i + 1) * 512)
            P.emit("pool", lambda o, sl=sl: o.dma_start(xbf_h[sl, :], x_in[sl, :]),
                   writes=["d_xbf"], kind="d", tag="xcast")

        A.mark()
        wq = A.alloc("wq", 16 * 1024)
        wk = A.alloc("wk", 16 * 1024)
        wv = A.alloc("wv", 16 * 1024)
        xT = [A.alloc(f"xT{r}", 16 * 512) for r in range(2)]
        for part, reg, rname in ((0, wq, "wq"), (1, wk, "wk"), (2, wv, "wv")):
            dst = bf(reg).rearrange("p (c n) -> p c n", n=1024)
            src = wqkv_in.rearrange("(c p) n -> p c n", p=128)[:, :, part * 1024:(part + 1) * 1024]
            for cc4 in range(4):
                csl = slice(cc4 * 4, cc4 * 4 + 4)
                P.emit("pool", lambda o, d=dst[:, csl, :], s=src[:, csl, :]: o.dma_start(d, s),
                       writes=[rname], kind="d", tag="ld_wqkv")

        def xT_ap(r, d, tsl=slice(0, 512)):
            a = xT[r][0]
            return arena.ap()[:, a + d * 512:a + (d + 1) * 512][:, tsl]

        stg_n = [0]

        def stage_slot():
            i = stg_n[0] % 4
            stg_n[0] += 1
            return stage.ap()[:, i * 512:(i + 1) * 512], f"stg{i}"

        if PH >= 0:
            for sti in range(NST):
                r = sti % 2
                t0 = sti * STW
                for d in range(16):
                    P.emit("sp", lambda o, d=d, r=r, t0=t0: o.dma_start(
                        xT_ap(r, d), xbf_h[t0:t0 + STW, d * 128:(d + 1) * 128], transpose=True),
                        reads=["d_xbf"], writes=[f"xT{r}"], kind="d", tag="xT")
                for wname, wreg, dest, dreg, scale in (
                        ("wk", wk, kT_h, "d_k", 1.0), ("wq", wq, qT_h, "d_q", QSCALE)):
                    for h in range(HPC):
                        bank = h % 4
                        for d in range(16):
                            wap = bf(wreg)[:, d * 1024 + h * 128: d * 1024 + (h + 1) * 128]
                            P.emit("pe", lambda o, w=wap, r=r, d=d, bank=bank, s=(d == 0), e=(d == 15):
                                   o.matmul(ps(bank), w, xT_ap(r, d), start=s, stop=e),
                                   reads=[wname, f"xT{r}"], writes=[PSR[bank]], tag="kqproj")
                        sap, sreg = stage_slot()
                        P.emit("act", lambda o, sap=sap, bank=bank, sc=scale:
                               o.activation(sap, ps(bank), AF.Copy, scale=sc),
                               reads=[PSR[bank]], writes=[sreg], tag="kqevac")
                        P.emit("sp", lambda o, sap=sap, h=h, t0=t0, dst=dest:
                               o.dma_start(dst[h, :, t0:t0 + STW], sap),
                               reads=[sreg], writes=[dreg], kind="d", tag="kqst")
                for sub in range(4):
                    for vch in range(2):
                        bank = 4 + ((sub * 2 + vch) % 4)
                        tsl = slice(sub * 128, (sub + 1) * 128)
                        for d in range(16):
                            P.emit("pe", lambda o, r=r, d=d, tsl=tsl, vch=vch, bank=bank, s=(d == 0), e=(d == 15):
                                   o.matmul(ps(bank), xT_ap(r, d, tsl),
                                            bf(wv)[:, d * 1024 + vch * 512: d * 1024 + (vch + 1) * 512],
                                            start=s, stop=e),
                                   reads=["wv", f"xT{r}"], writes=[PSR[bank]], tag="vproj")
                        sap, sreg = stage_slot()
                        P.emit("act", lambda o, sap=sap, bank=bank:
                               o.activation(sap, ps(bank), AF.Copy),
                               reads=[PSR[bank]], writes=[sreg], tag="vevac")
                        P.emit("sp", lambda o, sap=sap, t0=t0, sub=sub, vch=vch:
                               o.dma_start(v_h[t0 + sub * 128: t0 + (sub + 1) * 128,
                                               vch * 512:(vch + 1) * 512], sap),
                               reads=[sreg], writes=["d_v"], kind="d", tag="vst")
        A.release()

        for t in ("qT", "kT", "v"):
            if t in taps:
                src = {"qT": qT_h, "kT": kT_h, "v": v_h}[t]
                dreg = {"qT": "d_q", "kT": "d_k", "v": "d_v"}[t]
                P.emit("sp", lambda o, s=src, d2=tap_ext[t]: o.dma_start(d2, s),
                       reads=[dreg], kind="d", tag="tap")

        A.mark()
        ctx_r = A.alloc("ctx", HPC * 4096)
        A.mark()
        kb_ = [A.alloc(f"kb{r}", 4096) for r in range(2)]
        vb_ = [A.alloc(f"vb{r}", 4096) for r in range(2)]
        qb_ = [A.alloc(f"qb{r}", 4096) for r in range(2)]
        Pb_ = [A.alloc(f"P{r}", 4096) for r in range(2)]
        PT_r = A.alloc("PT", 32 * 512)

        if PH >= 1:
            for h in range(HPC):
                r = h % 2
                P.emit("sp", lambda o, h=h, r=r: o.dma_start(bf(kb_[r]), kT_h[h]),
                       reads=["d_k"], writes=[f"kb{r}"], kind="d", tag="kld")
                P.emit("sp", lambda o, h=h, r=r: o.dma_start(bf(qb_[r]), qT_h[h]),
                       reads=["d_q"], writes=[f"qb{r}"], kind="d", tag="qld")
                vsrc = v_h[:, h * 128:(h + 1) * 128].rearrange("(t p) n -> p t n", p=128)
                vdst = bf(vb_[r]).rearrange("p (t n) -> p t n", n=128)
                for q4 in range(4):
                    tsl = slice(q4 * 8, q4 * 8 + 8)
                    P.emit("sp", lambda o, s=vsrc[:, tsl, :], d2=vdst[:, tsl, :]: o.dma_start(d2, s),
                           reads=["d_v"], writes=[f"vb{r}"], kind="d", tag="vld")

                for s_ in range(NST):
                    NB = s_ + 1
                    Q0 = s_ * STW
                    scb = [0, 1, 6, 7]
                    cxb = 4 + (s_ % 2)

                    def pb_scores(j):
                        jp = j % 2
                        for kb in range(NB):
                            bank = scb[(j * NB + kb) % 4]
                            P.emit("pe", lambda o, j=j, kb=kb, bank=bank, r=r, Q0=Q0:
                                   o.matmul(ps(bank),
                                            bf(qb_[r])[:, Q0 + j * 128: Q0 + (j + 1) * 128],
                                            bf(kb_[r])[:, kb * 512:(kb + 1) * 512],
                                            start=True, stop=True),
                                   reads=[f"qb{r}", f"kb{r}"], writes=[PSR[bank]], tag="sc")
                            if kb == NB - 1:
                                P.emit("dve", lambda o, bank=bank, j=j:
                                       o.scalar_tensor_tensor(ps(bank), ps(bank), 1.0,
                                                              maskc[:, j * 512:(j + 1) * 512],
                                                              ALU.mult, ALU.add),
                                       reads=[PSR[bank], "maskc"], writes=[PSR[bank]], tag="mask")
                            P.emit("act", lambda o, bank=bank, jp=jp, kb=kb, j=j:
                                   o.activation(bf(Pb_[jp])[:, kb * 512:(kb + 1) * 512],
                                                ps(bank), AF.Exp, bias=cbias[:, 0:1],
                                                accum_out=sumblk[:, j * 8 + kb: j * 8 + kb + 1]),
                                   reads=[PSR[bank], "cbias"], writes=[f"P{jp}", "sumblk"], tag="exp")

                    def pb_post(j):
                        jp = j % 2
                        P.emit("dve", lambda o, j=j, NB=NB:
                               o.reduce_sum(rt[:, 128 + j: 129 + j], sumblk[:, j * 8: j * 8 + NB], axis=AX.X),
                               reads=["sumblk"], writes=["rt"], tag="psum")
                        P.emit("dve", lambda o, j=j:
                               o.reciprocal(rt[:, 132 + j: 133 + j], rt[:, 128 + j: 129 + j]),
                               reads=["rt"], writes=["rt"], tag="prec")
                        P.emit("dve", lambda o, j=j, jp=jp, NB=NB:
                               o.tensor_scalar_mul(bf(Pb_[jp])[:, 0:NB * 512],
                                                   bf(Pb_[jp])[:, 0:NB * 512],
                                                   rt[:, 132 + j: 133 + j]),
                               reads=["rt", f"P{jp}"], writes=[f"P{jp}"], tag="pscale")

                    def pb_tr(j):
                        jp = j % 2
                        for kbq in range(NB):
                            bank = 2 + (kbq % 2)
                            for ii in range(4):
                                kb = kbq * 4 + ii
                                P.emit("pe", lambda o, jp=jp, kb=kb, bank=bank, ii=ii:
                                       o.transpose(psb4(bank)[:, ii * 128:(ii + 1) * 128],
                                                   bf(Pb_[jp])[:, kb * 128:(kb + 1) * 128],
                                                   ident[:, :]),
                                       reads=[f"P{jp}", "ident"], writes=[PSR[bank]], tag="tr")
                            dst = bf(PT_r).rearrange("p (k n) -> p k n", n=512)
                            P.emit("dve", lambda o, j=j, kbq=kbq, bank=bank, dst=dst:
                                   o.tensor_copy(
                                       dst[:, kbq * 4:(kbq + 1) * 4, j * 128:(j + 1) * 128],
                                       psb4(bank).rearrange("p (k n) -> p k n", n=128)),
                                   reads=[PSR[bank]], writes=["PT"], tag="trev")

                    def pb_av():
                        for kb in range(4 * NB):
                            P.emit("pe", lambda o, kb=kb, r=r, s4=(kb == 0), e4=(kb == 4 * NB - 1), bank=cxb:
                                   o.matmul(ps(bank),
                                            bf(vb_[r])[:, kb * 128:(kb + 1) * 128],
                                            bf(PT_r)[:, kb * 512:(kb + 1) * 512],
                                            start=s4, stop=e4),
                                   reads=[f"vb{r}", "PT"], writes=[PSR[cxb]], tag="av")
                        P.emit("dve", lambda o, h=h, Q0=Q0, bank=cxb:
                               o.tensor_copy(bf(ctx_r)[:, h * 4096 + Q0: h * 4096 + Q0 + STW],
                                             ps(bank)),
                               reads=[PSR[cxb]], writes=["ctx"], tag="ctxev")

                    pb_scores(0)
                    pb_scores(1)
                    pb_post(0)
                    pb_tr(0)
                    pb_scores(2)
                    pb_post(1)
                    pb_tr(1)
                    pb_scores(3)
                    pb_post(2)
                    pb_tr(2)
                    pb_post(3)
                    pb_tr(3)
                    pb_av()
        A.release()

        if "ctx" in taps:
            for h in range(HPC):
                P.emit("sp", lambda o, h=h: o.dma_start(
                    tap_ext["ctx"][h * 128:(h + 1) * 128, :], bf(ctx_r)[:, h * 4096:(h + 1) * 4096]),
                    reads=["ctx"], kind="d", tag="tapctx")

        if PH >= 2:
            for h in range(HPC):
                P.emit("sp", lambda o, h=h: o.dma_start(
                    ccin.ap()[h * 128:(h + 1) * 128, :],
                    bf(ctx_r)[:, h * 4096:(h + 1) * 4096].bitcast(F32)),
                    reads=["ctx"], writes=["d_ccin"], kind="d", tag="ccst")
            P.emit("pool", lambda o: o.collective_compute(
                "AllGather", ALU.bypass,
                replica_groups=[[0, 1], [2, 3], [4, 5], [6, 7]],
                ins=[ccin.ap().opt()], outs=[ccout.ap().opt()]),
                reads=["d_ccin"], writes=["d_ccout"], kind="cc", tag="ccgather")

            def regsetup(o):
                pid = o.partition_id()
                o.reg_mov(r_po, pid)
                o.reg_mod(r_po, r_po, 2)
                o.reg_mul(r_po, r_po, 1024)
                o.reg_mov(r_po2, 1024 * 2048)
                return o.reg_add(r_po2, r_po2, r_po)
            P.emit("pool", regsetup, tag="regs")
        A.release()

        A.mark()
        ctx2_r = A.alloc("ctx2", 16 * 2048)
        if PH >= 2:
            ctx2f = bf(ctx2_r).bitcast(F32).rearrange("p (c n) -> p c n", n=1024)
            for half, roff in ((0, r_po), (1, r_po2)):
                src = bass.AP(ccout, roff, [[2048, 128], [128 * 2048, 8], [1, 1024]])
                P.emit("pool", lambda o, d=ctx2f[:, half * 8:(half + 1) * 8, :], s=src:
                       o.dma_start(d, s),
                       reads=["d_ccout"], writes=["ctx2"], kind="d", tag="ccld")
        if "ctx2" in taps:
            P.emit("sp", lambda o: o.dma_start(
                tap_ext["ctx2"].rearrange("(c p) n -> p c n", p=128),
                bf(ctx2_r).rearrange("p (c n) -> p c n", n=2048)),
                reads=["ctx2"], kind="d", tag="tapctx2")

        if PH >= 3:
            wout_r = A.alloc("wout", 16 * 2048)
            xs_ = [A.alloc(f"xs{r}", 2 * 2048) for r in range(2)]
            h1_ = [A.alloc(f"h1{r}", 2 * 2048) for r in range(2)]
            y_ = [A.alloc(f"yb{r}", 2 * 2048) for r in range(2)]
            dstw = bf(wout_r).rearrange("p (c n) -> p c n", n=2048)
            srcw = wout_in.rearrange("(c p) n -> p c n", p=128)
            for cc4 in range(4):
                csl = slice(cc4 * 4, cc4 * 4 + 4)
                P.emit("pool", lambda o, d=dstw[:, csl, :], s=srcw[:, csl, :]: o.dma_start(d, s),
                       writes=["wout"], kind="d", tag="ld_wout")
            for tt in range(16):
                r = tt % 2
                grp = 0 if r == 0 else 4
                banks = [PSR[grp + i] for i in range(4)]
                P.emit("sp", lambda o, tt=tt, r=r: o.dma_start(
                    f32(xs_[r]), xown_in[tt * 128:(tt + 1) * 128, :]),
                    writes=[f"xs{r}"], kind="d", tag="xld")
                for d in range(16):
                    for ct in range(4):
                        P.emit("pe", lambda o, d=d, ct=ct, tt=tt, grp=grp, s=(d == 0), e=(d == 15):
                               o.matmul(ps(grp + ct),
                                        bf(ctx2_r)[:, d * 2048 + tt * 128: d * 2048 + (tt + 1) * 128],
                                        bf(wout_r)[:, d * 2048 + ct * 512: d * 2048 + (ct + 1) * 512],
                                        start=s, stop=e),
                               reads=["ctx2", "wout"], writes=[banks[ct]], tag="oproj")
                P.emit("dve", lambda o, r=r, grp=grp:
                       o.scalar_tensor_tensor(f32(h1_[r]), ps(grp, 4), 1.0, f32(xs_[r]),
                                              ALU.mult, ALU.add, accum_out=stats[:, 0:1]),
                       reads=banks + [f"xs{r}"], writes=[f"h1{r}", "lnstats"], tag="h1")
                ln_chain(f32(h1_[r]), f"h1{r}", f32(y_[r]), f"yb{r}")
                P.emit("sp", lambda o, tt=tt, r=r: o.dma_start(
                    y_h[tt * 128:(tt + 1) * 128, :], f32(y_[r])),
                    reads=[f"yb{r}"], writes=["d_y"], kind="d", tag="yst")
                P.emit("pool", lambda o, tt=tt, r=r: o.dma_start(
                    ybf_h[tt * 128:(tt + 1) * 128, :], f32(y_[r])),
                    reads=[f"yb{r}"], writes=["d_ybf"], kind="d", tag="ybfst")
        A.release()

        if "y" in taps:
            P.emit("sp", lambda o: o.dma_start(tap_ext["y"], y_h),
                   reads=["d_y"], kind="d", tag="tapy")

        A.mark()
        yT_r = A.alloc("yT", 16 * 2048)
        if PH >= 4:
            gateb_r = A.alloc("gateb", 16 * 32)
            for d in range(16):
                P.emit("sp", lambda o, d=d: o.dma_start(
                    bf(yT_r)[:, d * 2048:(d + 1) * 2048],
                    ybf_h[:, d * 128:(d + 1) * 128], transpose=True),
                    reads=["d_ybf"], writes=["yT"], kind="d", tag="yT")
            P.emit("pool", lambda o: o.dma_start(
                bf(gateb_r).rearrange("p (c n) -> p c n", n=32),
                gate_in.rearrange("(c p) n -> p c n", p=128)),
                writes=["gateb"], kind="d", tag="ld_gate")
            for tt in range(16):
                bank = 6 + tt % 2
                for d in range(16):
                    P.emit("pe", lambda o, d=d, tt=tt, bank=bank, s=(d == 0), e=(d == 15):
                           o.matmul(ps(bank)[:, 0:32],
                                    bf(yT_r)[:, d * 2048 + tt * 128: d * 2048 + (tt + 1) * 128],
                                    bf(gateb_r)[:, d * 32:(d + 1) * 32],
                                    start=s, stop=e),
                           reads=["yT", "gateb"], writes=[PSR[bank]], tag="rlog")
                lg = rt[:, 0:32]
                rp = rt[:, 32:64]
                t8 = rt[:, 64:72]
                zap = rt[:, 96:128]
                P.emit("dve", lambda o, bank=bank, lg=lg: o.tensor_copy(lg, ps(bank)[:, 0:32]),
                       reads=[PSR[bank]], writes=["rt"], tag="rcp")
                P.emit("dve", lambda o, lg=lg: o.reduce_max(rt[:, 72:73], lg, axis=AX.X, negate=True),
                       reads=["rt"], writes=["rt"], tag="rmax")
                P.emit("act", lambda o, lg=lg, rp=rp: o.activation(
                    rp, lg, AF.Exp, bias=rt[:, 72:73], accum_out=rt[:, 73:74]),
                    reads=["rt"], writes=["rt"], tag="rexp")
                P.emit("dve", lambda o: o.reciprocal(rt[:, 74:75], rt[:, 73:74]),
                       reads=["rt"], writes=["rt"], tag="rrec")
                P.emit("dve", lambda o, rp=rp: o.tensor_scalar_mul(rp, rp, rt[:, 74:75]),
                       reads=["rt"], writes=["rt"], tag="rnorm")
                P.emit("dve", lambda o, rp=rp, t8=t8: o.max(out=t8, in_=rp),
                       reads=["rt"], writes=["rt"], tag="rtop8")
                P.emit("dve", lambda o: o.memset(rt[:, 68:72], -1e30),
                       reads=["rt"], writes=["rt"], tag="rmemset")
                P.emit("dve", lambda o, rp=rp, t8=t8, zap=zap: o.match_replace(
                    out=zap, in_to_replace=t8, in_values=rp, imm_value=0.0),
                    reads=["rt"], writes=["rt"], tag="rzap")
                P.emit("dve", lambda o, rp=rp, zap=zap: o.tensor_sub(rt[:, 128:160], rp, zap),
                       reads=["rt"], writes=["rt"], tag="rsub")
                P.emit("dve", lambda o: o.reduce_sum(rt[:, 75:76], rt[:, 64:68], axis=AX.X),
                       reads=["rt"], writes=["rt"], tag="rsum4")
                P.emit("dve", lambda o: o.reciprocal(rt[:, 76:77], rt[:, 75:76]),
                       reads=["rt"], writes=["rt"], tag="rrec4")
                P.emit("dve", lambda o, tt=tt: o.tensor_scalar_mul(
                    RW[:, tt * 32:(tt + 1) * 32], rt[:, 128:160], rt[:, 76:77]),
                    reads=["rt"], writes=["RW"], tag="rwst")
        if "rw" in taps:
            P.emit("sp", lambda o: o.dma_start(
                tap_ext["rw"].rearrange("(t p) n -> p t n", p=128),
                RW.ap().rearrange("p (t n) -> p t n", n=32)),
                reads=["RW"], kind="d", tag="taprw")

        if PH >= 5:
            wgu_ = [A.alloc(f"wgu{r}", 8 * 1536) for r in range(2)]
            sg_ = [A.alloc(f"sg{r}", 2 * 2048) for r in range(2)]
            gt_ = [A.alloc(f"gt{r}", 6 * 2048) for r in range(2)]
            for e in range(NE):
                for half in range(2):
                    slot = half
                    dstw2 = bf(wgu_[slot]).rearrange("p (c n) -> p c n", n=1536)
                    srcw2 = gup_in[e].rearrange("(c p) n -> p c n", p=128)
                    for q in range(2):
                        csl = slice(q * 4, q * 4 + 4)
                        ssl = slice(half * 8 + q * 4, half * 8 + q * 4 + 4)
                        P.emit("pool", lambda o, d=dstw2[:, csl, :], s=srcw2[:, ssl, :]:
                               o.dma_start(d, s),
                               writes=[f"wgu{slot}"], kind="d", tag="ld_wgu")
                gc_order = [0, 1, 6, 7, 2, 3, 8, 9, 4, 5, 10, 11]
                for gc in gc_order:
                    bank0 = (gc % 2) * 4
                    for d in range(16):
                        slot = d // 8
                        dl = d % 8
                        for mt in range(4):
                            P.emit("pe", lambda o, gc=gc, dl=dl, d=d, mt=mt, slot=slot, bank0=bank0,
                                   s=(d == 0), e2=(d == 15):
                                   o.matmul(ps(bank0 + mt),
                                            bf(wgu_[slot])[:, dl * 1536 + gc * 128: dl * 1536 + (gc + 1) * 128],
                                            bf(yT_r)[:, d * 2048 + mt * 512: d * 2048 + (mt + 1) * 512],
                                            start=s, stop=e2),
                                   reads=[f"wgu{slot}", "yT"], writes=[PSR[bank0 + mt]], tag="gu")
                    if gc < 6:
                        for mt in range(4):
                            P.emit("act", lambda o, gc=gc, mt=mt, bank0=bank0:
                                   o.activation(f32(sg_[gc % 2])[:, mt * 512:(mt + 1) * 512],
                                                ps(bank0 + mt), AF.Silu),
                                   reads=[PSR[bank0 + mt]], writes=[f"sg{gc % 2}"], tag="silu")
                    else:
                        i = gc - 6
                        for mt in range(4):
                            P.emit("dve", lambda o, i=i, mt=mt, gc=gc, e=e, bank0=bank0:
                                   o.tensor_mul(bf(gt_[e % 2])[:, i * 2048 + mt * 512: i * 2048 + (mt + 1) * 512],
                                                f32(sg_[gc % 2])[:, mt * 512:(mt + 1) * 512],
                                                ps(bank0 + mt)),
                                   reads=[PSR[bank0 + mt], f"sg{gc % 2}"],
                                   writes=[f"gt{e % 2}"], tag="gmul")
                gdst = gated_h[e].rearrange("(c p) n -> p c n", p=128)
                P.emit("sp", lambda o, e=e, gdst=gdst:
                       o.dma_start(gdst, bf(gt_[e % 2]).rearrange("p (c n) -> p c n", n=2048)),
                       reads=[f"gt{e % 2}"], writes=[f"d_g{e}"], kind="d", tag="gst")
        A.release()
        if "gated" in taps:
            for e in range(2):
                P.emit("sp", lambda o, e=e: o.dma_start(tap_ext["gated"][e], gated_h[e]),
                       reads=[f"d_g{e}"], kind="d", tag="tapg")

        A.mark()
        acc_r = A.alloc("acc", 16 * 2 * 2048)

        def acc_ap(tt, csl=slice(0, 2048)):
            a = acc_r[0]
            return arena.ap()[:, a + tt * 4096: a + (tt + 1) * 4096].bitcast(F32)[:, csl]

        if PH >= 6:
            A.mark()
            gl_ = [A.alloc(f"gl{r}", 3 * 2048) for r in range(2)]
            wd_ = [A.alloc(f"wd{r}", 3 * 2048) for r in range(2)]
            for tt in range(16):
                P.emit("dve", lambda o, tt=tt: o.memset(acc_ap(tt), 0.0),
                       writes=[f"acc{tt}"], tag="accz")
            for e in range(NE):
                for half in range(2):
                    slot = half
                    gsrc = gated_h[e].rearrange("(c p) n -> p c n", p=128)
                    P.emit("sp", lambda o, slot=slot, s=gsrc[:, half * 3:half * 3 + 3, :]:
                           o.dma_start(bf(gl_[slot]).rearrange("p (c n) -> p c n", n=2048), s),
                           reads=[f"d_g{e}"], writes=[f"gl{slot}"], kind="d", tag="gld")
                    wsrc = gdn_in[e].rearrange("(c p) n -> p c n", p=128)
                    P.emit("pool", lambda o, slot=slot, s=wsrc[:, half * 3:half * 3 + 3, :]:
                           o.dma_start(bf(wd_[slot]).rearrange("p (c n) -> p c n", n=2048), s),
                           writes=[f"wd{slot}"], kind="d", tag="ld_wd")
                for tt in range(16):
                    grp = (tt % 2) * 4
                    for i in range(6):
                        slot = i // 3
                        il = i % 3
                        for ct in range(4):
                            P.emit("pe", lambda o, il=il, ct=ct, tt=tt, slot=slot, grp=grp,
                                   s=(i == 0), e2=(i == 5):
                                   o.matmul(ps(grp + ct),
                                            bf(gl_[slot])[:, il * 2048 + tt * 128: il * 2048 + (tt + 1) * 128],
                                            bf(wd_[slot])[:, il * 2048 + ct * 512: il * 2048 + (ct + 1) * 512],
                                            start=s, stop=e2),
                                   reads=[f"gl{slot}", f"wd{slot}"], writes=[PSR[grp + ct]], tag="dn")
                    for ct in range(4):
                        P.emit("dve", lambda o, tt=tt, ct=ct, e=e, grp=grp:
                               o.scalar_tensor_tensor(
                                   acc_ap(tt, slice(ct * 512, (ct + 1) * 512)),
                                   ps(grp + ct), RW[:, tt * 32 + e: tt * 32 + e + 1],
                                   acc_ap(tt, slice(ct * 512, (ct + 1) * 512)),
                                   ALU.mult, ALU.add),
                               reads=[PSR[grp + ct], "RW", f"acc{tt}"], writes=[f"acc{tt}"], tag="dacc")
            A.release()
        if "acc" in taps:
            for tt in range(16):
                P.emit("sp", lambda o, tt=tt: o.dma_start(
                    tap_ext["acc"][tt * 128:(tt + 1) * 128, :], acc_ap(tt)),
                    reads=[f"acc{tt}"], kind="d", tag="tapacc")

        if PH >= 7:
            yl_ = [A.alloc(f"yl{r}", 2 * 2048) for r in range(2)]
            h2_ = [A.alloc(f"h2{r}", 2 * 2048) for r in range(2)]
            ob_ = [A.alloc(f"ob{r}", 2 * 2048) for r in range(2)]
            for tt in range(16):
                r = tt % 2
                P.emit("sp", lambda o, tt=tt, r=r: o.dma_start(
                    f32(yl_[r]), y_h[tt * 128:(tt + 1) * 128, :]),
                    reads=["d_y"], writes=[f"yl{r}"], kind="d", tag="yld")
                P.emit("dve", lambda o, tt=tt, r=r:
                       o.scalar_tensor_tensor(f32(h2_[r]), acc_ap(tt), 1.0,
                                              f32(yl_[r]), ALU.mult, ALU.add,
                                              accum_out=stats[:, 0:1]),
                       reads=[f"acc{tt}", f"yl{r}"], writes=[f"h2{r}", "lnstats"], tag="h2")
                ln_chain(f32(h2_[r]), f"h2{r}", f32(ob_[r]), f"ob{r}")
                P.emit("sp", lambda o, tt=tt, r=r: o.dma_start(
                    out_ext[tt * 128:(tt + 1) * 128, :], f32(ob_[r])),
                    reads=[f"ob{r}"], writes=["d_out"], kind="d", tag="outst")
        A.release()

        totals = P.finalize()

        def replay(e):
            def go(engobj):
                for op in P.streams[e]:
                    for (snm, val) in op.waits:
                        engobj.wait_ge(sems[snm], val)
                    ins = op.fn(engobj)
                    if op.needs_inc and ins is not None:
                        ins.then_inc(sems[op.inc_sem], 16 if op.kind == "d" else 1)
                if e == "sp":
                    engobj.wait_ge(sems["sd"], totals["sd"])
                    if totals["pd"]:
                        engobj.wait_ge(sems["pd"], totals["pd"])
            return go

        block.tensor(replay("pe"))
        block.vector(replay("dve"))
        block.scalar(replay("act"))
        block.gpsimd(replay("pool"))
        block.sync(replay("sp"))

    return nc


_CACHE = {}


def make_in_maps(L0_attn_x, L0_attn_w_qkv, L0_attn_w_out, L0_ffn_gate,
                 L0_ffn_gate_up_proj, L0_ffn_down_proj):
    x = np.ascontiguousarray(np.asarray(L0_attn_x, dtype=np.float32))
    wqkv = np.asarray(L0_attn_w_qkv, dtype=np.float32)
    wout = np.ascontiguousarray(np.asarray(L0_attn_w_out, dtype=np.float32))
    gate = np.ascontiguousarray(np.asarray(L0_ffn_gate, dtype=np.float32))
    gup = np.ascontiguousarray(np.asarray(L0_ffn_gate_up_proj, dtype=np.float32))
    gdn = np.ascontiguousarray(np.asarray(L0_ffn_down_proj, dtype=np.float32))
    ident, maskc = _host_consts()
    in_maps = []
    for c in range(8):
        b, p = c // 2, c % 2
        wq = wqkv[:, p * 1024:(p + 1) * 1024]
        wk = wqkv[:, 2048 + p * 1024: 2048 + (p + 1) * 1024]
        wv = wqkv[:, 4096 + p * 1024: 4096 + (p + 1) * 1024]
        in_maps.append({
            "x": np.ascontiguousarray(x[b]),
            "xown": np.ascontiguousarray(x[b, p * 2048:(p + 1) * 2048]),
            "wqkv": np.ascontiguousarray(np.concatenate([wq, wk, wv], axis=1)),
            "wout": wout,
            "gate": gate,
            "gup": gup,
            "gdn": gdn,
            "ident": ident,
            "maskc": maskc,
        })
    return in_maps


def kernel(L0_attn_x, L0_attn_w_qkv, L0_attn_w_out, L0_ffn_gate,
           L0_ffn_gate_up_proj, L0_ffn_down_proj):
    if "nc" not in _CACHE:
        _CACHE["nc"] = build_program()
    nc = _CACHE["nc"]
    in_maps = make_in_maps(L0_attn_x, L0_attn_w_qkv, L0_attn_w_out, L0_ffn_gate,
                           L0_ffn_gate_up_proj, L0_ffn_down_proj)
    res = run_bass_kernel_spmd(nc, in_maps, core_ids=list(range(8)))
    out = np.empty((B, S, D), dtype=np.float32)
    for c in range(8):
        b, p = c // 2, c % 2
        out[b, p * 2048:(p + 1) * 2048] = res.results[c]["out"]
    return out
